# version 01
# baseline (speedup 1.0000x reference)
import numpy as np
import ml_dtypes
import concourse.bass as bass
import concourse.mybir as mybir
from concourse.bass_utils import run_bass_kernel_spmd

F32 = mybir.dt.float32
F32R = mybir.dt.float32r
BF16 = mybir.dt.bfloat16
AF = mybir.ActivationFunctionType
ALU = mybir.AluOpType
AX = mybir.AxisListType.X

_DEBUG = False
_EL1 = None
NCORE = 8
TOK = 1024
D = 2048
NCH = 16
NTOK = 8192
DSL = 256
LN_EPS = 1e-5
BF = ml_dtypes.bfloat16



def _host_consts():
    c = {}
    c["ones_col"] = np.ones((128, 1), np.float32)
    c["ones_row"] = np.ones((1, 128), np.float32)
    c["ident"] = np.eye(128, dtype=np.float32)
    p = np.arange(128)
    c["m16e"] = (p[None, :] // 16 == np.arange(8)[:, None]).astype(np.float32)
    oh8 = np.zeros((8, 8 * 128), np.float32)
    for e in range(8):
        oh8[e, e * 128:(e + 1) * 128] = 1.0
    c["oh8"] = oh8
    e_b = np.zeros((32, 128), np.float32)
    e_c = np.zeros((32, 128), np.float32)
    for k in range(16):
        e_b[k] = (p % 16 == k)
        e_c[k + 16] = (p % 16 == k)
    c["e_b"], c["e_c"] = e_b, e_c
    g = np.zeros((16, 128, 128), np.float32)
    for gl in range(16):
        for pp in range(128):
            g[gl, gl * 8 + pp // 16, pp] = 1.0
    c["g_pat"] = np.ascontiguousarray(g.reshape(16 * 128, 128))
    c["r_pat"] = (p[:, None] // 16 == np.arange(8)[None, :]).astype(np.float32)
    return c


def _prep(inputs):
    h = dict(_host_consts())
    f32 = np.float32
    x = np.asarray(inputs["L0_attn_x"], f32).reshape(NTOK, D)
    h["xT"] = np.ascontiguousarray(x.T)

    dw = np.asarray(inputs["L0_ffn_down_w"], f32)
    h["dnT"] = np.ascontiguousarray(dw.transpose(2, 0, 1).reshape(D, 128))
    uw = np.asarray(inputs["L0_ffn_up_w"], f32)
    h["upT"] = np.ascontiguousarray(uw.transpose(0, 2, 1).reshape(128, D))
    h["rtT"] = np.ascontiguousarray(np.asarray(inputs["L0_ffn_router_w"], f32).T)

    h["xprojT"] = np.ascontiguousarray(np.asarray(inputs["L0_attn_x_proj_w"], f32).T)
    dtT = np.ascontiguousarray(np.asarray(inputs["L0_attn_dt_proj_w"], f32).T)
    h["dtprojT_sl"] = [np.ascontiguousarray(dtT[:, c * DSL:(c + 1) * DSL]) for c in range(NCORE)]
    dtb = np.asarray(inputs["L0_attn_dt_proj_b"], f32)
    h["dtb_sl"] = [np.ascontiguousarray(dtb[c * DSL:(c + 1) * DSL].reshape(2, 128).T)
                   for c in range(NCORE)]
    h["D_sb"] = np.ascontiguousarray(np.asarray(inputs["L0_attn_D"], f32).reshape(16, 128).T)
    al = np.asarray(inputs["L0_attn_A_log"], f32)
    alp = np.zeros((NCORE, 128, 32), f32)
    for core in range(NCORE):
        for g in range(32):
            for pp in range(128):
                alp[core, pp, g] = al[core * DSL + g * 8 + pp // 16, pp % 16]
    h["alogp"] = alp

    for nm in ["wq", "wk", "wv", "wo"]:
        h[nm + "T"] = np.ascontiguousarray(np.asarray(inputs["L1_attn_" + nm], f32).T)
    h["wmqT"] = np.ascontiguousarray(np.asarray(inputs["L1_attn_wmap_q"], f32).T)
    h["wmkT"] = np.ascontiguousarray(np.asarray(inputs["L1_attn_wmap_k"], f32).T)
    for nm in ["bq", "bk", "bo"]:
        h[nm] = np.ascontiguousarray(
            np.asarray(inputs["L1_attn_" + nm], f32).reshape(16, 128).T)
    h["bvR"] = np.asarray(inputs["L1_attn_bv"], f32).reshape(1, D)

    w1 = np.asarray(inputs["L1_ffn_w1"]); w3 = np.asarray(inputs["L1_ffn_w3"])
    w2 = np.asarray(inputs["L1_ffn_w2"])
    h["w1T"] = np.ascontiguousarray(
        np.concatenate([w1, np.asarray(inputs["L1_ffn_sw1"])[None]], 0).transpose(0, 2, 1))
    h["w3T"] = np.ascontiguousarray(
        np.concatenate([w3, np.asarray(inputs["L1_ffn_sw3"])[None]], 0).transpose(0, 2, 1))
    h["w2T"] = np.ascontiguousarray(
        np.concatenate([w2, np.asarray(inputs["L1_ffn_sw2"])[None]], 0).transpose(0, 2, 1))
    h["gateT"] = np.ascontiguousarray(np.asarray(inputs["L1_ffn_gate_w"], f32).T)
    h["gate_b"] = np.asarray(inputs["L1_ffn_gate_b"], f32).reshape(1, 8)
    return h



class Sem:
    def __init__(self, nc, name):
        self.h = nc.semaphore(name).__enter__()
        self.n = 0

    def inc(self, k=1):
        self.n += k
        return self.h


class Ctx:
    def __init__(self, nc):
        self.nc = nc
        self._lp = nc.allow_low_precision(reason="fp32r matmul operand tiles")
        self._lp.__enter__()
        self.P = [nc.alloc_psum_tensor(f"psP{i}", [128, 1024], F32) for i in range(4)]
        self.jC = nc.alloc_sbuf_tensor("ctx_jnk", [128, 64], F32)
        self.sems = {}

    def sem(self, name):
        if name not in self.sems:
            self.sems[name] = Sem(self.nc, name)
        return self.sems[name]


def sb(nc, name, shape, dt):
    return nc.alloc_sbuf_tensor(name, shape, dt)


def dmai(eng, out, in_, s):
    eng.dma_start(out=out, in_=in_).then_inc(s.inc(16), 16)


def _consts_load(C, sy, s_ld, names):
    nc = C.nc
    t = {}
    if "ones_col" in names:
        d = nc.dram_tensor("ones_col", [128, 1], F32, kind="ExternalInput").ap()
        t["ones_col"] = sb(nc, "c_onescol", [128, 1], F32R)
        dmai(sy, t["ones_col"][:], d.bitcast(F32R), s_ld)
    if "ones_row" in names:
        d = nc.dram_tensor("ones_row", [1, 128], F32, kind="ExternalInput").ap()
        t["ones_row"] = sb(nc, "c_onesrow", [1, 128], F32R)
        dmai(sy, t["ones_row"][:], d.bitcast(F32R), s_ld)
    if "ident" in names:
        d = nc.dram_tensor("ident", [128, 128], F32, kind="ExternalInput").ap()
        t["ident"] = sb(nc, "c_ident", [128, 128], F32)
        dmai(sy, t["ident"][:], d, s_ld)
    return t


def emit_ln(C, X, OUT, consts, pfx, out_bitcast_f32_in=True):
    nc = C.nc
    P = C.P
    import contextlib
    s_sq = C.sem("ln_sq")
    s_sqf = C.sem("ln_sqf")
    b_sq, b_sqf = s_sq.n, s_sqf.n
    _les = contextlib.ExitStack()
    _e = _les.enter_context
    sq = [_e(nc.sbuf_tensor(f"{pfx}_sq{i}", [128, 1024], F32R)) for i in range(2)]
    m_sb = _e(nc.sbuf_tensor(f"{pfx}_m", [1, 1024], F32R))
    r_sb = _e(nc.sbuf_tensor(f"{pfx}_r", [1, 1024], F32R))
    t_sb = _e(nc.sbuf_tensor(f"{pfx}_t", [1, 1024], F32))
    v_sb = _e(nc.sbuf_tensor(f"{pfx}_v", [1, 1024], F32))
    sd_sb = _e(nc.sbuf_tensor(f"{pfx}_sd", [1, 1024], F32))
    nt = _e(nc.sbuf_tensor(f"{pfx}_nt", [128, 1024], F32))

    with nc.Block() as blk:
        @blk.scalar
        def _(sc):
            for c in range(NCH):
                if c >= 2:
                    sc.wait_ge(s_sqf.h, b_sqf + c - 1)
                sc.activation(sq[c % 2][:], X[c][:].bitcast(F32), AF.Square) \
                    .then_inc(s_sq.inc(), 1)

        @blk.tensor
        def _(tn):
            for c in range(NCH):
                for hh in range(2):
                    cl = slice(hh * 512, (hh + 1) * 512)
                    tn.matmul(P[2][0:1, cl], consts["ones_col"][:], X[c][:, cl],
                              start=(c == 0), stop=(c == NCH - 1))
                tn.wait_ge(s_sq.h, b_sq + c + 1)
                for hh in range(2):
                    cl = slice(hh * 512, (hh + 1) * 512)
                    ins = tn.matmul(P[3][0:1, cl], consts["ones_col"][:], sq[c % 2][:, cl],
                                    start=(c == 0), stop=(c == NCH - 1))
                ins.then_inc(s_sqf.inc(), 1)

    with nc.Block() as blk:
        @blk.vector
        def _(v):
            v.tensor_scalar(m_sb[:], P[2][0:1, :], 1.0 / D, None, ALU.mult)
            v.tensor_scalar(v_sb[:], P[3][0:1, :], 1.0 / D, None, ALU.mult)
            v.tensor_tensor(t_sb[:], m_sb[:].bitcast(F32), m_sb[:].bitcast(F32), ALU.mult)
            v.tensor_tensor(v_sb[:], v_sb[:], t_sb[:], ALU.subtract)
            v.tensor_scalar(v_sb[:], v_sb[:], LN_EPS, None, ALU.add)
            v.memset(C.jC[:], 0.0)

    with nc.Block() as blk:
        @blk.scalar
        def _(sc):
            sc.activation(sd_sb[:], v_sb[:], AF.Sqrt)
            sc.memzero(C.jC[:])

    with nc.Block() as blk:
        @blk.vector
        def _(v):
            v.reciprocal(r_sb[:], sd_sb[:])
            v.memset(C.jC[:], 0.0)
            v.memset(C.jC[:], 0.0)

    with nc.Block() as blk:
        @blk.tensor
        def _(tn):
            for hh in range(2):
                cl = slice(hh * 512, (hh + 1) * 512)
                tn.matmul(P[2][:, cl], consts["ones_row"][:], m_sb[0:1, cl],
                          start=True, stop=True)
                tn.matmul(P[3][:, cl], consts["ones_row"][:], r_sb[0:1, cl],
                          start=True, stop=True)

    with nc.Block() as blk:
        @blk.vector
        def _(v):
            for c in range(NCH):
                v.tensor_tensor(nt[:], X[c][:].bitcast(F32), P[2][:], ALU.subtract)
                v.tensor_tensor(OUT[c][:], nt[:], P[3][:], ALU.mult)
            v.memset(C.jC[:], 0.0)
    _les.close()


def emit_store_stream(C, X, d_y, s_st, waits=()):
    nc = C.nc
    with nc.Block() as blk:
        @blk.sync
        def _(sy):
            for s, val in waits:
                sy.wait_ge(s.h, val)
            yv = d_y.rearrange("(c p) t -> c p t", p=128)
            for c in range(NCH):
                dmai(sy, yv[c], X[c][:].bitcast(F32), s_st)
            sy.wait_ge(s_st.h, s_st.n)


def emit_load_stream(C, X, d_x, sy, s_ld):
    xv = d_x.rearrange("(c p) t -> c p t", p=128)
    for c in range(NCH):
        dmai(sy, X[c][:], xv[c].bitcast(F32R), s_ld)



def build_l0ffn():
    nc = bass.Bass("TRN2", target_bir_lowering=False)
    C = Ctx(nc)
    P = C.P
    d_x = nc.dram_tensor("xT", [D, TOK], F32, kind="ExternalInput").ap()
    d_y = nc.dram_tensor("yT", [D, TOK], F32, kind="ExternalOutput").ap()
    d_dn = nc.dram_tensor("dnT", [D, 128], F32, kind="ExternalInput").ap()
    d_up = nc.dram_tensor("upT", [128, D], F32, kind="ExternalInput").ap()
    d_rt = nc.dram_tensor("rtT", [D, 8], F32, kind="ExternalInput").ap()
    d_m16 = nc.dram_tensor("m16e", [8, 128], F32, kind="ExternalInput").ap()

    X = [sb(nc, f"x{c}", [128, 1024], F32R) for c in range(NCH)]
    XN = [sb(nc, f"xn{c}", [128, 1024], F32R) for c in range(NCH)]
    dn_sb = sb(nc, "dn", [128, 16 * 128], F32R)
    up_sb = sb(nc, "up", [128, D], F32R)
    rt_sb = sb(nc, "rt", [128, 16 * 8], F32R)
    m16_sb = sb(nc, "m16", [8, 128], F32R)
    a_sb = sb(nc, "a", [128, 1024], F32R)
    ag_sb = sb(nc, "ag", [128, 1024], F32R)
    lg_sb = sb(nc, "lg", [8, 1024], F32)
    sct = sb(nc, "sct", [128, 64], F32)
    et = sb(nc, "et", [128, 64], F32)
    ex_t = sb(nc, "ext", [128, 64], F32)
    mx8 = sb(nc, "mx8", [128, 8], F32)
    sm8 = sb(nc, "sm8", [128, 8], F32)
    r8 = sb(nc, "r8", [128, 8], F32)
    jnk = sb(nc, "jnk", [128, 8], F32)
    dbg_tr = sb(nc, "dbgtr", [128, 64], F32) if _DEBUG else None
    dbg_mx = sb(nc, "dbgmx", [128, 8], F32) if _DEBUG else None
    dbg_mx2 = sb(nc, "dbgmx2", [128, 8], F32) if _DEBUG else None
    m1 = sb(nc, "m1", [128, 64], F32)
    m2 = sb(nc, "m2", [128, 64], F32)
    sup = sb(nc, "sup", [128, 64], F32)
    gate_t = sb(nc, "gatet", [128, 64], F32)
    gate_e = sb(nc, "gatee", [8, 1024], F32R)

    s_ld = C.sem("ld")
    s_st = C.sem("st")
    s_up = C.sem("sup")
    s_upf = C.sem("supf")

    with nc.Block() as blk:
        @blk.sync
        def _(sy):
            emit_load_stream(C, X, d_x, sy, s_ld)
            dmai(sy, dn_sb[:].rearrange("p (c m) -> p c m", c=16),
                 d_dn.rearrange("(c p) m -> p c m", p=128).bitcast(F32R), s_ld)
            dmai(sy, up_sb[:], d_up.bitcast(F32R), s_ld)
            dmai(sy, rt_sb[:].rearrange("p (c m) -> p c m", c=16),
                 d_rt.rearrange("(c p) m -> p c m", p=128).bitcast(F32R), s_ld)
            dmai(sy, m16_sb[:], d_m16.bitcast(F32R), s_ld)
            C.consts = _consts_load(C, sy, s_ld, ["ones_col", "ones_row", "ident"])
            sy.wait_ge(s_ld.h, s_ld.n)

    emit_ln(C, X, XN, C.consts, "ln0")

    with nc.Block() as blk:
        @blk.tensor
        def _(tn):
            for c in range(NCH):
                for hh in range(2):
                    cl = slice(hh * 512, (hh + 1) * 512)
                    tn.matmul(P[0][0:8, cl], rt_sb[:, c * 8:(c + 1) * 8], XN[c][:, cl],
                              start=(c == 0), stop=(c == NCH - 1))
                    tn.matmul(P[1][:, cl], dn_sb[:, c * 128:(c + 1) * 128], XN[c][:, cl],
                              start=(c == 0), stop=(c == NCH - 1))

    with nc.Block() as blk:
        @blk.scalar
        def _(sc):
            sc.activation(a_sb[:], P[1][:], AF.Silu)
            sc.memzero(C.jC[:])

        @blk.vector
        def _(v):
            v.tensor_copy(lg_sb[:], P[0][0:8, :])
            v.memset(C.jC[:], 0.0)

    with nc.Block() as blk:
        @blk.tensor
        def _(tn):
            for t8 in range(8):
                tn.transpose(P[2][:, t8 * 8:(t8 + 1) * 8],
                             lg_sb[0:8, t8 * 128:(t8 + 1) * 128],
                             C.consts["ident"][0:8, 0:8])

    S8 = [slice(t8 * 8, (t8 + 1) * 8) for t8 in range(8)]
    with nc.Block() as blk:
        @blk.vector
        def _(v):
            s_hz = C.sem("hz")
            v.tensor_copy(sct[:], P[2][:, 0:64])
            if _DEBUG:
                v.tensor_copy(dbg_tr[:], sct[:])
            for t8 in range(8):
                v.reduce_max(mx8[:, t8:t8 + 1], sct[:, S8[t8]], axis=AX) \
                    .then_inc(s_hz.inc(), 1)
            v.wait_ge(s_hz.h, s_hz.n)
            for t8 in range(8):
                v.tensor_scalar(et[:, S8[t8]], sct[:, S8[t8]], mx8[:, t8:t8 + 1],
                                None, ALU.subtract)
            v.memset(jnk[:], 0.0)

    with nc.Block() as blk:
        @blk.scalar
        def _(sc):
            sc.activation(ex_t[:], et[:], AF.Exp)
            sc.memzero(C.jC[:])

    with nc.Block() as blk:
        @blk.vector
        def _(v):
            s_hz = C.sem("hz")
            for t8 in range(8):
                v.reduce_sum(sm8[:, t8:t8 + 1], ex_t[:, S8[t8]], axis=AX) \
                    .then_inc(s_hz.inc(), 1)
            v.wait_ge(s_hz.h, s_hz.n)
            v.reciprocal(r8[:], sm8[:]).then_inc(s_hz.inc(), 1)
            v.wait_ge(s_hz.h, s_hz.n)
            for t8 in range(8):
                v.tensor_scalar(sct[:, S8[t8]], ex_t[:, S8[t8]], r8[:, t8:t8 + 1],
                                None, ALU.mult)
            for t8 in range(8):
                v.reduce_max(mx8[:, t8:t8 + 1], sct[:, S8[t8]], axis=AX) \
                    .then_inc(s_hz.inc(), 1)
            v.wait_ge(s_hz.h, s_hz.n)
            for t8 in range(8):
                v.tensor_scalar(m1[:, S8[t8]], sct[:, S8[t8]], mx8[:, t8:t8 + 1],
                                None, ALU.is_ge)
            for t8 in range(8):
                v.scalar_tensor_tensor(sup[:, S8[t8]], m1[:, S8[t8]], -1e30,
                                       sct[:, S8[t8]], ALU.mult, ALU.add)
            for t8 in range(8):
                v.reduce_max(mx8[:, t8:t8 + 1], sup[:, S8[t8]], axis=AX) \
                    .then_inc(s_hz.inc(), 1)
            v.wait_ge(s_hz.h, s_hz.n)
            for t8 in range(8):
                v.tensor_scalar(m2[:, S8[t8]], sup[:, S8[t8]], mx8[:, t8:t8 + 1],
                                None, ALU.is_ge)
            for t8 in range(8):
                v.tensor_tensor(m1[:, S8[t8]], m1[:, S8[t8]], m2[:, S8[t8]], ALU.add)
            for t8 in range(8):
                v.tensor_tensor(gate_t[:, S8[t8]], sct[:, S8[t8]], m1[:, S8[t8]], ALU.mult)
            if _DEBUG:
                v.tensor_copy(dbg_mx2[:], mx8[:])
            v.memset(jnk[:], 0.0)

    with nc.Block() as blk:
        @blk.tensor
        def _(tn):
            for t8 in range(8):
                tn.transpose(P[2][0:8, t8 * 128:(t8 + 1) * 128],
                             gate_t[:, t8 * 8:(t8 + 1) * 8],
                             C.consts["ident"][:, 0:128])

    with nc.Block() as blk:
        @blk.vector
        def _(v):
            v.tensor_copy(gate_e[:], P[2][0:8, :])
            v.memset(C.jC[:], 0.0)

    with nc.Block() as blk:
        @blk.tensor
        def _(tn):
            for hh in range(2):
                cl = slice(hh * 512, (hh + 1) * 512)
                tn.matmul(P[3][:, cl], m16_sb[:], gate_e[0:8, cl], start=True, stop=True)

    with nc.Block() as blk:
        @blk.vector
        def _(v):
            v.tensor_tensor(ag_sb[:], a_sb[:].bitcast(F32), P[3][:], ALU.mult)
            v.memset(C.jC[:], 0.0)

    with nc.Block() as blk:
        @blk.tensor
        def _(tn):
            for m in range(NCH):
                if m >= 2:
                    tn.wait_ge(s_upf.h, m - 1)
                for hh in range(2):
                    cl = slice(hh * 512, (hh + 1) * 512)
                    ins = tn.matmul(P[m % 2][:, cl], up_sb[:, m * 128:(m + 1) * 128],
                                    ag_sb[:, cl], start=True, stop=True)
                ins.then_inc(s_up.inc(), 1)

        @blk.vector
        def _(v):
            for m in range(NCH):
                v.wait_ge(s_up.h, m + 1)
                v.scalar_tensor_tensor(X[m][:], P[m % 2][:], 0.5, X[m][:].bitcast(F32),
                                       ALU.mult, ALU.add).then_inc(s_upf.inc(), 1)

    if _DEBUG:
        d_dbg = {nm: nc.dram_tensor("dbg_" + nm, shp, F32, kind="ExternalOutput").ap()
                 for nm, shp in [("xn", [128, 1024]), ("lg", [8, 1024]),
                                 ("sct", [128, 64]), ("et", [128, 64]),
                                 ("tr", [128, 64]), ("mx", [128, 8]),
                                 ("ex", [128, 64]), ("r8", [128, 8]), ("sm", [128, 8]),
                                 ("mx2", [128, 8]),
                                 ("gt", [128, 64]),
                                 ("ge", [8, 1024]), ("a", [128, 1024]),
                                 ("ag", [128, 1024])]}
        with nc.Block() as blk:
            @blk.sync
            def _(sy):
                dmai(sy, d_dbg["xn"], XN[0][:].bitcast(F32), s_st)
                dmai(sy, d_dbg["lg"], lg_sb[:], s_st)
                dmai(sy, d_dbg["sct"], sct[:], s_st)
                dmai(sy, d_dbg["et"], et[:], s_st)
                dmai(sy, d_dbg["tr"], dbg_tr[:], s_st)
                dmai(sy, d_dbg["mx"], dbg_mx[:], s_st)
                dmai(sy, d_dbg["ex"], ex_t[:], s_st)
                dmai(sy, d_dbg["r8"], r8[:], s_st)
                dmai(sy, d_dbg["sm"], sm8[:], s_st)
                dmai(sy, d_dbg["mx2"], dbg_mx2[:], s_st)
                dmai(sy, d_dbg["gt"], gate_t[:], s_st)
                dmai(sy, d_dbg["ge"], gate_e[:].bitcast(F32), s_st)
                dmai(sy, d_dbg["a"], a_sb[:].bitcast(F32), s_st)
                dmai(sy, d_dbg["ag"], ag_sb[:].bitcast(F32), s_st)
                sy.wait_ge(s_st.h, s_st.n)
    emit_store_stream(C, X, d_y, s_st, waits=[(s_upf, s_upf.n)])
    return nc, ["xT", "dnT", "upT", "rtT", "m16e", "ones_col", "ones_row", "ident"]



NEXP = 9
NI = 11


def build_l1ffn():
    nc = bass.Bass("TRN2", target_bir_lowering=False)
    C = Ctx(nc)
    P = C.P
    d_x = nc.dram_tensor("xT", [D, TOK], F32, kind="ExternalInput").ap()
    d_y = nc.dram_tensor("yT", [D, TOK], F32, kind="ExternalOutput").ap()
    d_w1 = nc.dram_tensor("w1T", [NEXP, D, 1408], BF16, kind="ExternalInput").ap()
    d_w3 = nc.dram_tensor("w3T", [NEXP, D, 1408], BF16, kind="ExternalInput").ap()
    d_w2 = nc.dram_tensor("w2T", [NEXP, 1408, D], BF16, kind="ExternalInput").ap()
    d_gt = nc.dram_tensor("gateT", [D, 8], F32, kind="ExternalInput").ap()
    d_gb = nc.dram_tensor("gate_b", [1, 8], F32, kind="ExternalInput").ap()
    d_oh8 = nc.dram_tensor("oh8", [8, 8 * 128], F32, kind="ExternalInput").ap()

    X = [sb(nc, f"x{c}", [128, 1024], F32R) for c in range(NCH)]
    XNb = [sb(nc, f"xnb{c}", [128, 1024], BF16) for c in range(NCH)]
    gt_sb = sb(nc, "gt", [128, 16 * 8], BF16)
    gb_sb = sb(nc, "gb", [1, 8], F32R)
    oh8_sb = sb(nc, "oh8s", [8, 8 * 128], F32R)
    sc_sb = sb(nc, "scs", [8, 1024], F32)
    sct = sb(nc, "sct", [128, 64], F32)
    ab = sb(nc, "ab", [128, 64], F32)
    lb = sb(nc, "lb", [128, 64], F32)
    ssq = sb(nc, "ssq", [128, 64], F32)
    bt = sb(nc, "bt", [128, 64], F32)
    m1 = sb(nc, "m1", [128, 64], F32)
    m2 = sb(nc, "m2", [128, 64], F32)
    sup = sb(nc, "sup", [128, 64], F32)
    mx8 = sb(nc, "mx8", [128, 8], F32)
    cw_t = sb(nc, "cwt", [128, 64], F32)
    cwe_sb = sb(nc, "cwe", [8, 1024], F32R)
    cwb_sb = sb(nc, "cwb", [128, 1024], F32)
    th = [sb(nc, f"th{i}", [128, 1024], F32) for i in range(2)]
    hbt = sb(nc, "hbt", [128, 1024], F32)
    hb = [sb(nc, f"hb{i}", [128, 1024], BF16) for i in range(NI)]
    w13 = [sb(nc, f"w13_{i}", [128, 16 * 128], BF16) for i in range(4)]
    w2b = [sb(nc, f"w2_{i}", [128, NI * 128], BF16) for i in range(2)]

    s_ld = C.sem("ld")
    s_st = C.sem("st")
    s_wl = C.sem("wl")
    s_wf13 = C.sem("wf13")
    s_wf2 = C.sem("wf2")
    s_h1 = C.sem("h1")
    s_h3 = C.sem("h3")
    s_sl = C.sem("sl")
    s_tt1 = C.sem("tt1")
    s_hb = C.sem("shb")
    s_o = C.sem("so")
    s_ox = C.sem("sox")
    s_cwm = C.sem("cwm")
    s_cwb = C.sem("scwb")

    with nc.Block() as blk:
        @blk.sync
        def _(sy):
            emit_load_stream(C, X, d_x, sy, s_ld)
            dmai(sy, gb_sb[:], d_gb.bitcast(F32R), s_ld)
            dmai(sy, oh8_sb[:], d_oh8.bitcast(F32R), s_ld)
            C.consts = _consts_load(C, sy, s_ld, ["ones_col", "ones_row", "ident"])
            sy.wait_ge(s_ld.h, s_ld.n)

        @blk.gpsimd
        def _(g):
            g.dma_start(out=gt_sb[:].rearrange("p (c m) -> p c m", c=16),
                        in_=d_gt.rearrange("(c p) m -> p c m", p=128)).then_inc(s_ld.inc(16), 16)
            g.wait_ge(s_ld.h, s_ld.n)

    emit_ln(C, X, XNb, C.consts, "ln1")

    with nc.Block() as blk:
        @blk.tensor
        def _(tn):
            for c in range(NCH):
                for hh in range(2):
                    cl = slice(hh * 512, (hh + 1) * 512)
                    tn.matmul(P[0][0:8, cl], gt_sb[:, c * 8:(c + 1) * 8], XNb[c][:, cl],
                              start=(c == 0), stop=(c == NCH - 1))

    with nc.Block() as blk:
        @blk.vector
        def _(v):
            v.tensor_copy(sc_sb[:], P[0][0:8, :])
            v.memset(C.jC[:], 0.0)

    with nc.Block() as blk:
        @blk.tensor
        def _(tn):
            for t8 in range(8):
                tn.transpose(P[1][:, t8 * 8:(t8 + 1) * 8],
                             sc_sb[0:8, t8 * 128:(t8 + 1) * 128],
                             C.consts["ident"][0:8, 0:8])
            tn.matmul(P[2][:, 0:8], C.consts["ones_row"][:], gb_sb[:], start=True, stop=True)

    with nc.Block() as blk:
        @blk.vector
        def _(v):
            v.tensor_copy(sct[:], P[1][:, 0:64])
            v.scalar_tensor_tensor(ab[:], sct[:], -1.0, sct[:], ALU.mult, ALU.max)
            v.memset(C.jC[:], 0.0)

    with nc.Block() as blk:
        @blk.scalar
        def _(sc):
            sc.activation(lb[:], ab[:], AF.Exp, scale=-1.0)
            sc.memzero(C.jC[:])

    with nc.Block() as blk:
        @blk.scalar
        def _(sc):
            sc.activation(ab[:], lb[:], AF.Ln, bias=1.0)
            sc.memzero(C.jC[:])

    with nc.Block() as blk:
        @blk.vector
        def _(v):
            v.scalar_tensor_tensor(lb[:], sct[:], 0.0, ab[:], ALU.max, ALU.add)
            v.memset(C.jC[:], 0.0)

    with nc.Block() as blk:
        @blk.scalar
        def _(sc):
            sc.activation(ssq[:], lb[:], AF.Sqrt)
            sc.memzero(C.jC[:])

    S8 = [slice(t8 * 8, (t8 + 1) * 8) for t8 in range(8)]
    with nc.Block() as blk:
        @blk.vector
        def _(v):
            s_hz = C.sem("hz")
            for t8 in range(8):
                v.tensor_tensor(bt[:, S8[t8]], ssq[:, S8[t8]], P[2][:, 0:8], ALU.add)
            for t8 in range(8):
                v.reduce_max(mx8[:, t8:t8 + 1], bt[:, S8[t8]], axis=AX) \
                    .then_inc(s_hz.inc(), 1)
            v.wait_ge(s_hz.h, s_hz.n)
            for t8 in range(8):
                v.tensor_scalar(m1[:, S8[t8]], bt[:, S8[t8]], mx8[:, t8:t8 + 1],
                                None, ALU.is_ge)
            for t8 in range(8):
                v.scalar_tensor_tensor(sup[:, S8[t8]], m1[:, S8[t8]], -1e30,
                                       bt[:, S8[t8]], ALU.mult, ALU.add)
            for t8 in range(8):
                v.reduce_max(mx8[:, t8:t8 + 1], sup[:, S8[t8]], axis=AX) \
                    .then_inc(s_hz.inc(), 1)
            v.wait_ge(s_hz.h, s_hz.n)
            for t8 in range(8):
                v.tensor_scalar(m2[:, S8[t8]], sup[:, S8[t8]], mx8[:, t8:t8 + 1],
                                None, ALU.is_ge)
            for t8 in range(8):
                v.tensor_tensor(m1[:, S8[t8]], m1[:, S8[t8]], m2[:, S8[t8]], ALU.add)
            for t8 in range(8):
                v.scalar_tensor_tensor(cw_t[:, S8[t8]], m1[:, S8[t8]], 0.5,
                                       ssq[:, S8[t8]], ALU.mult, ALU.mult)
            v.memset(C.jC[:], 0.0)

    with nc.Block() as blk:
        @blk.tensor
        def _(tn):
            for t8 in range(8):
                tn.transpose(P[1][0:8, t8 * 128:(t8 + 1) * 128],
                             cw_t[:, t8 * 8:(t8 + 1) * 8],
                             C.consts["ident"][:, 0:128])

    with nc.Block() as blk:
        @blk.vector
        def _(v):
            v.tensor_copy(cwe_sb[:], P[1][0:8, :])
            v.memset(C.jC[:], 0.0)

    elist = list(range(NEXP)) if _EL1 is None else list(_EL1)
    nrouted_before = [sum(1 for x in elist[:k] if x < 8) for k in range(len(elist) + 1)]
    with nc.Block() as blk:
        @blk.sync
        def _(sy):
            for ei, e in enumerate(elist):
                for i in range(NI):
                    for wd, nm in ((d_w1, 0), (d_w3, 1)):
                        j13 = ei * 22 + 2 * i + nm
                        if j13 >= 4:
                            sy.wait_ge(s_wf13.h, j13 - 3)
                        dmai(sy, w13[j13 % 4][:].rearrange("p (c m) -> p c m", c=16),
                             wd[e, :, i * 128:(i + 1) * 128]
                             .rearrange("(c p) m -> p c m", p=128), s_wl)
                for dd in range(NCH):
                    j2 = ei * 16 + dd
                    if j2 >= 2:
                        sy.wait_ge(s_o.h, j2 - 1)
                    dmai(sy, w2b[j2 % 2][:].rearrange("p (c m) -> p c m", c=NI),
                         d_w2[e, :, dd * 128:(dd + 1) * 128]
                         .rearrange("(c p) m -> p c m", p=128), s_wl)

        @blk.tensor
        def _(tn):
            jg = 0
            for ei, e in enumerate(elist):
                routed = e < 8
                if routed:
                    if ei > 0:
                        tn.wait_ge(s_hb.h, 11 * ei)
                        tn.wait_ge(s_ox.h, 16 * ei)
                    for hh in range(2):
                        cl = slice(hh * 512, (hh + 1) * 512)
                        ins = tn.matmul(P[2][:, cl], oh8_sb[0:8, e * 128:(e + 1) * 128],
                                        cwe_sb[0:8, cl], start=True, stop=True)
                    ins.then_inc(s_cwm.inc(), 1)
                elif ei > 0:
                    tn.wait_ge(s_hb.h, 11 * ei)
                    tn.wait_ge(s_ox.h, 16 * ei)
                for i in range(NI):
                    for nm in range(2):
                        tn.wait_ge(s_wl.h, 16 * (jg + 1))
                        slot = w13[(ei * 22 + 2 * i + nm) % 4]
                        dst = P[2 * (i % 2) + nm]
                        if i >= 2 and nm == 0:
                            tn.wait_ge(s_sl.h, 11 * ei + i - 1)
                            tn.wait_ge(s_tt1.h, 11 * ei + i - 1)
                        if i == 1 and nm == 0 and routed:
                            tn.wait_ge(s_cwb.h, nrouted_before[ei] + 1)
                        for k in range(NCH):
                            for hh in range(2):
                                cl = slice(hh * 512, (hh + 1) * 512)
                                ins = tn.matmul(dst[:, cl], slot[:, k * 128:(k + 1) * 128],
                                                XNb[k][:, cl], start=(k == 0), stop=(k == NCH - 1))
                        ins.then_inc(s_wf13.inc(), 1)
                        jg += 1
                tn.wait_ge(s_hb.h, 11 * (ei + 1))
                for dd in range(NCH):
                    tn.wait_ge(s_wl.h, 16 * (jg + 1))
                    if dd >= 2:
                        tn.wait_ge(s_ox.h, 16 * ei + dd - 1)
                    slot = w2b[(ei * 16 + dd) % 2]
                    dst = P[dd % 2]
                    for k in range(NI):
                        for hh in range(2):
                            cl = slice(hh * 512, (hh + 1) * 512)
                            ins = tn.matmul(dst[:, cl], slot[:, k * 128:(k + 1) * 128],
                                            hb[k][:, cl], start=(k == 0), stop=(k == NI - 1))
                    ins.then_inc(s_o.inc(), 1)
                    jg += 1

        @blk.scalar
        def _(sc):
            for ei, e in enumerate(elist):
                for i in range(NI):
                    gi = 11 * ei + i
                    sc.wait_ge(s_wf13.h, 22 * ei + 2 * i + 1)
                    if gi >= 2:
                        sc.wait_ge(s_tt1.h, gi - 1)
                    sc.activation(th[gi % 2][:], P[2 * (i % 2)][:], AF.Silu) \
                        .then_inc(s_sl.inc(), 1)

        @blk.vector
        def _(v):
            for ei, e in enumerate(elist):
                routed = e < 8
                if routed:
                    v.wait_ge(s_cwm.h, nrouted_before[ei] + 1)
                    v.tensor_copy(cwb_sb[:], P[2][:]).then_inc(s_cwb.inc(), 1)
                for i in range(NI):
                    gi = 11 * ei + i
                    v.wait_ge(s_sl.h, gi + 1)
                    v.wait_ge(s_wf13.h, 22 * ei + 2 * i + 2)
                    v.tensor_tensor(hbt[:], th[gi % 2][:], P[2 * (i % 2) + 1][:],
                                    ALU.mult).then_inc(s_tt1.inc(), 1)
                    if routed:
                        v.tensor_tensor(hb[i][:], hbt[:], cwb_sb[:], ALU.mult) \
                            .then_inc(s_hb.inc(), 1)
                    else:
                        v.tensor_scalar(hb[i][:], hbt[:], 0.5, None, ALU.mult) \
                            .then_inc(s_hb.inc(), 1)
                for dd in range(NCH):
                    v.wait_ge(s_o.h, 16 * ei + dd + 1)
                    v.tensor_tensor(X[dd][:], X[dd][:].bitcast(F32), P[dd % 2][:],
                                    ALU.add).then_inc(s_ox.inc(), 1)

    if _DEBUG:
        d_cwe = nc.dram_tensor("dbg_cwe", [8, 1024], F32, kind="ExternalOutput").ap()
        d_ssq = nc.dram_tensor("dbg_ssq", [128, 64], F32, kind="ExternalOutput").ap()
        d_hb0 = nc.dram_tensor("dbg_hb0", [128, 1024], BF16, kind="ExternalOutput").ap()
        d_xnb0 = nc.dram_tensor("dbg_xnb0", [128, 1024], BF16, kind="ExternalOutput").ap()
        with nc.Block() as blk:
            @blk.sync
            def _(sy):
                sy.wait_ge(s_ox.h, s_ox.n)
                dmai(sy, d_cwe, cwe_sb[:].bitcast(F32), s_st)
                dmai(sy, d_ssq, ssq[:], s_st)
                dmai(sy, d_hb0, hb[0][:], s_st)
                dmai(sy, d_xnb0, XNb[0][:], s_st)
                sy.wait_ge(s_st.h, s_st.n)
    emit_store_stream(C, X, d_y, s_st, waits=[(s_ox, s_ox.n)])
    return nc, ["xT", "w1T", "w3T", "w2T", "gateT", "gate_b", "oh8",
                "ones_col", "ones_row", "ident"]



def build_l1attn():
    import contextlib
    nc = bass.Bass("TRN2", target_bir_lowering=False)
    C = Ctx(nc)
    P = C.P
    d_x = nc.dram_tensor("xT", [D, TOK], F32, kind="ExternalInput").ap()
    d_y = nc.dram_tensor("yT", [D, TOK], F32, kind="ExternalOutput").ap()
    d_wq = nc.dram_tensor("wqT", [D, D], F32, kind="ExternalInput").ap()
    d_wk = nc.dram_tensor("wkT", [D, D], F32, kind="ExternalInput").ap()
    d_wv = nc.dram_tensor("wvT", [D, D], F32, kind="ExternalInput").ap()
    d_wo = nc.dram_tensor("woT", [D, D], F32, kind="ExternalInput").ap()
    d_wmq = nc.dram_tensor("wmqT", [128, 64], F32, kind="ExternalInput").ap()
    d_wmk = nc.dram_tensor("wmkT", [128, 64], F32, kind="ExternalInput").ap()
    d_bq = nc.dram_tensor("bq", [128, 16], F32, kind="ExternalInput").ap()
    d_bk = nc.dram_tensor("bk", [128, 16], F32, kind="ExternalInput").ap()
    d_bo = nc.dram_tensor("bo", [128, 16], F32, kind="ExternalInput").ap()
    d_bv = nc.dram_tensor("bvR", [1, D], F32, kind="ExternalInput").ap()
    ar_in = nc.dram_tensor("ar_in", [128, 1040], F32)
    ar_out = nc.dram_tensor("ar_out", [128, 1040], F32)

    X = [sb(nc, f"x{c}", [128, 1024], F32R) for c in range(NCH)]
    qe = [sb(nc, f"qe{j}", [128, 1024], F32R) for j in range(8)]
    ke = [sb(nc, f"ke{j}", [128, 1024], BF16) for j in range(8)]
    kv_sb = sb(nc, "kv", [128, 1024], F32R)
    ks_bd = sb(nc, "ksbd", [128, 16], F32R)
    kst = sb(nc, "kst", [128, 8], F32)
    rcp_t = sb(nc, "rcpt", [1, 1024], F32)
    rcp = sb(nc, "rcp", [1, 1024], F32R)
    wmq_sb = sb(nc, "wmq", [128, 64], BF16)
    wmk_sb = sb(nc, "wmk", [128, 64], BF16)
    bq_sb = sb(nc, "bqs", [128, 16], F32)
    bk_sb = sb(nc, "bks", [128, 16], F32)
    bo_sb = sb(nc, "bos", [128, 16], F32)
    bv_bf = sb(nc, "bvbf", [1, D], BF16)
    orow_bf = sb(nc, "orowbf", [1, 128], BF16)
    ident_bf = sb(nc, "identbf", [128, 128], BF16)

    s_ld = C.sem("ld"); s_st = C.sem("st")
    s_w = C.sem("sw"); s_wf = C.sem("swf")
    s_pj = C.sem("pj"); s_bh = C.sem("bh")
    s_qm = C.sem("qm"); s_e1 = C.sem("e1"); s_e2 = C.sem("e2"); s_elu = C.sem("elu")
    s_vm = C.sem("vm"); s_ve = C.sem("ve"); s_vw = C.sem("svw"); s_vwf = C.sem("svwf")
    s_tp = C.sem("tp"); s_te = C.sem("te")
    s_cc = C.sem("cc"); s_ar = C.sem("ar")
    s_nm = C.sem("nm"); s_nr = C.sem("nr")
    s_am = C.sem("am"); s_ae = C.sem("ae"); s_ad = C.sem("ad")
    s_om = C.sem("om"); s_oe = C.sem("oe"); s_od = C.sem("od")

    with nc.Block() as blk:
        @blk.sync
        def _(sy):
            emit_load_stream(C, X, d_x, sy, s_ld)
            for dst, srcd in ((bq_sb, d_bq), (bk_sb, d_bk), (bo_sb, d_bo)):
                dmai(sy, dst[:], srcd, s_ld)
            C.consts = _consts_load(C, sy, s_ld, ["ones_col", "ones_row", "ident"])
            sy.wait_ge(s_ld.h, s_ld.n)

        @blk.gpsimd
        def _(g):
            g.dma_start(out=wmq_sb[:], in_=d_wmq).then_inc(s_ld.inc(16), 16)
            g.dma_start(out=wmk_sb[:], in_=d_wmk).then_inc(s_ld.inc(16), 16)
            g.dma_start(out=bv_bf[:], in_=d_bv).then_inc(s_ld.inc(16), 16)
            g.wait_ge(s_ld.h, s_ld.n)

    with nc.Block() as blk:
        @blk.gpsimd
        def _(g):
            g.dma_start(out=orow_bf[:], in_=C.consts["ones_row"][:].bitcast(F32)) \
                .then_inc(s_ld.inc(16), 16)
            g.dma_start(out=ident_bf[:], in_=C.consts["ident"][:]).then_inc(s_ld.inc(16), 16)
            g.wait_ge(s_ld.h, s_ld.n)

        @blk.vector
        def _(v):
            v.memset(C.jC[:], 0.0)
            v.tensor_scalar(ks_bd[:], C.jC[:, 0:16], 0.0, None, ALU.mult)

    es_xn = contextlib.ExitStack()
    XNb = [es_xn.enter_context(nc.sbuf_tensor(f"xnb{c}", [128, 1024], BF16))
           for c in range(NCH)]

    emit_ln(C, X, XNb, C.consts, "lna")

    es_qk = contextlib.ExitStack()
    wr = [es_qk.enter_context(nc.sbuf_tensor(f"wr{i}", [128, 16 * 128], BF16))
          for i in range(4)]
    qh = es_qk.enter_context(nc.sbuf_tensor("qh", [128, 1024], BF16))
    el1 = es_qk.enter_context(nc.sbuf_tensor("el1", [128, 1024], F32))
    el2 = es_qk.enter_context(nc.sbuf_tensor("el2", [128, 1024], F32))

    for which, d_w, wmap, bias_sb, out_tiles in (
            ("q", d_wq, wmq_sb, bq_sb, qe), ("k", d_wk, wmk_sb, bk_sb, ke)):
        b_w, b_wf = s_w.n, s_wf.n
        b_pj, b_bh, b_qm = s_pj.n, s_bh.n, s_qm.n
        b_e1, b_e2, b_el = s_e1.n, s_e2.n, s_elu.n

        with nc.Block() as blk:
            @blk.gpsimd
            def _(g):
                for f in range(NCH):
                    if f >= 4:
                        g.wait_ge(s_pj.h, b_pj + f - 3)
                    g.dma_start(out=wr[f % 4][:].rearrange("p (c m) -> p c m", c=16),
                                in_=d_w[:, f * 128:(f + 1) * 128]
                                .rearrange("(c p) m -> p c m", p=128)) \
                        .then_inc(s_w.inc(16), 16)

            @blk.tensor
            def _(tn):
                for f in range(NCH):
                    j = f // 2
                    tn.wait_ge(s_w.h, b_w + 16 * (f + 1))
                    if f >= 2:
                        tn.wait_ge(s_bh.h, b_bh + f - 1)
                    for k in range(NCH):
                        for hh in range(2):
                            cl = slice(hh * 512, (hh + 1) * 512)
                            ins = tn.matmul(P[f % 2][:, cl],
                                            wr[f % 4][:, k * 128:(k + 1) * 128],
                                            XNb[k][:, cl], start=(k == 0), stop=(k == NCH - 1))
                    ins.then_inc(s_pj.inc(), 1)
                    tn.wait_ge(s_bh.h, b_bh + f + 1)
                    if f >= 4:
                        tn.wait_ge(s_elu.h, b_el + j - 1)
                    rows = slice(64 * (f % 2), 64 * (f % 2) + 64)
                    for hh in range(2):
                        cl = slice(hh * 512, (hh + 1) * 512)
                        ins = tn.matmul(P[2 + (j % 2)][rows, cl], wmap[:], qh[:, cl],
                                        start=True, stop=True)
                    ins.then_inc(s_qm.inc(), 1)

            @blk.scalar
            def _(sc):
                for f in range(NCH):
                    j = f // 2
                    sc.wait_ge(s_pj.h, b_pj + f + 1)
                    if f >= 1:
                        sc.wait_ge(s_qm.h, b_qm + f)
                    sc.activation(qh[:], P[f % 2][:], AF.Identity,
                                  bias=bias_sb[:, f:f + 1]).then_inc(s_bh.inc(), 1)
                    if f % 2 == 1:
                        sc.wait_ge(s_e1.h, b_e1 + j + 1)
                        if j >= 1:
                            sc.wait_ge(s_elu.h, b_el + j)
                        sc.activation(el2[:], el1[:], AF.Exp).then_inc(s_e2.inc(), 1)

            @blk.vector
            def _(v):
                for j in range(8):
                    v.wait_ge(s_qm.h, b_qm + 2 * j + 2)
                    if j >= 1:
                        v.wait_ge(s_e2.h, b_e2 + j)
                    v.tensor_scalar(el1[:], P[2 + (j % 2)][:], 0.0, None, ALU.min) \
                        .then_inc(s_e1.inc(), 1)
                    v.wait_ge(s_e2.h, b_e2 + j + 1)
                    v.scalar_tensor_tensor(out_tiles[j][:], P[2 + (j % 2)][:], 0.0,
                                           el2[:], ALU.max, ALU.add) \
                        .then_inc(s_elu.inc(), 1)
                    if which == "k":
                        v.reduce_sum(kst[:, j:j + 1], out_tiles[j][:], axis=AX) \
                            .then_inc(C.sem("hz").inc(), 1)
                v.memset(C.jC[:], 0.0)

    es_qk.close()

    with nc.Block() as blk:
        @blk.vector
        def _(v):
            v.wait_ge(C.sem("hz").h, C.sem("hz").n)
            for j in range(8):
                v.tensor_copy(ks_bd[0:64, 2 * j:2 * j + 1], kst[0:64, j:j + 1])
                v.tensor_copy(ks_bd[64:128, 2 * j + 1:2 * j + 2], kst[64:128, j:j + 1])
            v.memset(C.jC[:], 0.0)

    es_v = contextlib.ExitStack()
    vw = es_v.enter_context(nc.sbuf_tensor("vw", [128, 16 * 512], BF16))
    vT = [es_v.enter_context(nc.sbuf_tensor(f"vT{t}", [128, D], BF16)) for t in range(4)]
    emkT = [es_v.enter_context(nc.sbuf_tensor(f"emkT{t}", [128, 1024], BF16))
            for t in range(4)]

    for sblk in range(2):
        with nc.Block() as blk:
            @blk.gpsimd
            def _(g):
                for nq in range(4):
                    b = sblk * 4 + nq
                    if b >= 1:
                        g.wait_ge(s_vm.h, 4 * b)
                    for k in range(NCH):
                        g.dma_start(out=vw[:, k * 512:(k + 1) * 512],
                                    in_=d_wv[k * 128:(k + 1) * 128,
                                             nq * 512:(nq + 1) * 512]) \
                            .then_inc(s_vw.inc(16), 16)

            @blk.tensor
            def _(tn):
                for nq in range(4):
                    b = sblk * 4 + nq
                    tn.wait_ge(s_vw.h, 16 * 16 * (b + 1))
                    for tb in range(4):
                        tbg = sblk * 4 + tb
                        it = b * 4 + tb
                        slot = it % 4
                        dst = P[slot // 2][:, (slot % 2) * 512:(slot % 2) * 512 + 512]
                        if it >= 4:
                            tn.wait_ge(s_ve.h, it - 3)
                        for k in range(NCH):
                            tn.matmul(dst, XNb[k][:, tbg * 128:(tbg + 1) * 128],
                                      vw[:, k * 512:(k + 1) * 512],
                                      start=(k == 0), stop=False)
                        ins = tn.matmul(dst, orow_bf[:],
                                        bv_bf[:, nq * 512:(nq + 1) * 512],
                                        start=False, stop=True)
                        ins.then_inc(s_vm.inc(), 1)

            @blk.scalar
            def _(sc):
                for nq in range(4):
                    for tb in range(4):
                        it = (sblk * 4 + nq) * 4 + tb
                        slot = it % 4
                        srcp = P[slot // 2][:, (slot % 2) * 512:(slot % 2) * 512 + 512]
                        sc.wait_ge(s_vm.h, it + 1)
                        sc.activation(vT[tb][:, nq * 512:(nq + 1) * 512], srcp,
                                      AF.Identity).then_inc(s_ve.inc(), 1)

        with nc.Block() as blk:
            @blk.tensor
            def _(tn):
                for tb in range(4):
                    tbg = sblk * 4 + tb
                    for h in range(16):
                        it = (sblk * 4 + tb) * 16 + h
                        if it >= 2:
                            tn.wait_ge(s_te.h, it - 1)
                        j = h // 2
                        rows = slice(64 * (h % 2), 64 * (h % 2) + 64)
                        ib = 64 * (h % 2)
                        sl_c = (it % 2) * 1024
                        tn.transpose(P[2][:].bitcast(BF16)[:, sl_c:sl_c + 64],
                                     ke[j][rows, tbg * 128:(tbg + 1) * 128],
                                     ident_bf[ib:ib + 64, ib:ib + 64]).then_inc(s_tp.inc(), 1)
                    for h in range(16):
                        rows = slice(64 * (h % 2), 64 * (h % 2) + 64)
                        tn.wait_ge(s_te.h, (sblk * 4 + tb) * 16 + h + 1)
                        tn.matmul(P[3][rows, (h // 2) * 128:(h // 2) * 128 + 128],
                                  emkT[tb][:, h * 64:(h + 1) * 64],
                                  vT[tb][:, h * 128:(h + 1) * 128],
                                  start=(tbg == 0), stop=(tbg == 7))

            @blk.vector
            def _(v):
                for tb in range(4):
                    for h in range(16):
                        it = (sblk * 4 + tb) * 16 + h
                        v.wait_ge(s_tp.h, it + 1)
                        v.tensor_copy(emkT[tb][:, h * 64:(h + 1) * 64],
                                      P[2][:].bitcast(BF16)[:, (it % 2) * 1024:(it % 2) * 1024 + 64]) \
                            .then_inc(s_te.inc(), 1)

    with nc.Block() as blk:
        @blk.vector
        def _(v):
            v.tensor_copy(kv_sb[:], P[3][:])
            v.memset(C.jC[:], 0.0)

    es_v.close()
    es_xn.close()

    with nc.Block() as blk:
        @blk.sync
        def _(sy):
            dmai(sy, ar_in.ap()[:, 0:1024], kv_sb[:].bitcast(F32), s_ar)
            dmai(sy, ar_in.ap()[:, 1024:1040], ks_bd[:].bitcast(F32), s_ar)

        @blk.gpsimd
        def _(g):
            g.wait_ge(s_ar.h, 32)
            g.collective_compute(
                "AllReduce", ALU.add,
                replica_groups=[[0, 1, 2, 3], [4, 5, 6, 7]],
                ins=[ar_in.ap()], outs=[ar_out.ap()],
            ).then_inc(s_cc.inc(), 1)
            g.wait_ge(s_cc.h, 1)

    with nc.Block() as blk:
        @blk.sync
        def _(sy):
            dmai(sy, kv_sb[:], ar_out.ap()[:, 0:1024].bitcast(F32R), s_ar)
            dmai(sy, ks_bd[:], ar_out.ap()[:, 1024:1040].bitcast(F32R), s_ar)
            sy.wait_ge(s_ar.h, s_ar.n)

    es_att = contextlib.ExitStack()
    att = [es_att.enter_context(nc.sbuf_tensor(f"att{h}", [128, 1024], BF16))
           for h in range(NCH)]
    at_sb = es_att.enter_context(nc.sbuf_tensor("at", [128, 1024], F32))

    with nc.Block() as blk:
        @blk.tensor
        def _(tn):
            for h in range(16):
                if h >= 2:
                    tn.wait_ge(s_ad.h, h - 1)
                for hh in range(2):
                    cl = slice(hh * 512, (hh + 1) * 512)
                    ins = tn.matmul(P[2 + (h % 2)][0:1, cl], ks_bd[:, h:h + 1],
                                    qe[h // 2][:, cl], start=True, stop=True)
                ins.then_inc(s_nm.inc(), 1)
                if h >= 2:
                    tn.wait_ge(s_ae.h, h - 1)
                rows = slice(64 * (h % 2), 64 * (h % 2) + 64)
                for hh in range(2):
                    cl = slice(hh * 512, (hh + 1) * 512)
                    tn.matmul(P[h % 2][:, cl],
                              kv_sb[rows, (h // 2) * 128:(h // 2) * 128 + 128],
                              qe[h // 2][rows, cl], start=True, stop=True)
                tn.wait_ge(s_nr.h, h + 1)
                for hh in range(2):
                    cl = slice(hh * 512, (hh + 1) * 512)
                    ins = tn.matmul(P[2 + (h % 2)][:, cl], C.consts["ones_row"][:],
                                    rcp[0:1, cl], start=True, stop=True)
                ins.then_inc(s_am.inc(), 1)

        @blk.vector
        def _(v):
            s_hz = C.sem("hz")
            for h in range(16):
                v.wait_ge(s_nm.h, h + 1)
                v.tensor_scalar(rcp_t[:], P[2 + (h % 2)][0:1, :], 1e-6, None, ALU.add) \
                    .then_inc(s_hz.inc(), 1)
                v.wait_ge(s_hz.h, s_hz.n)
                v.reciprocal(rcp[:], rcp_t[:]).then_inc(s_nr.inc(), 1)
                v.wait_ge(s_ae.h, h + 1)
                v.tensor_tensor(att[h][:], at_sb[:], P[2 + (h % 2)][:], ALU.mult) \
                    .then_inc(s_ad.inc(), 1)

        @blk.scalar
        def _(sc):
            for h in range(16):
                sc.wait_ge(s_am.h, h + 1)
                if h >= 1:
                    sc.wait_ge(s_ad.h, h)
                sc.activation(at_sb[:], P[h % 2][:], AF.Identity).then_inc(s_ae.inc(), 1)

    wr2 = [es_att.enter_context(nc.sbuf_tensor(f"wr2_{i}", [128, 16 * 128], BF16))
           for i in range(4)]
    ot = es_att.enter_context(nc.sbuf_tensor("ot", [128, 1024], F32))
    with nc.Block() as blk:
        @blk.gpsimd
        def _(g):
            for dd in range(NCH):
                if dd >= 4:
                    g.wait_ge(s_om.h, dd - 3)
                g.dma_start(out=wr2[dd % 4][:].rearrange("p (c m) -> p c m", c=16),
                            in_=d_wo[:, dd * 128:(dd + 1) * 128]
                            .rearrange("(c p) m -> p c m", p=128)) \
                    .then_inc(s_w.inc(16), 16)

        @blk.tensor
        def _(tn):
            b_w = s_w.n - 16 * NCH
            for dd in range(NCH):
                tn.wait_ge(s_w.h, b_w + 16 * (dd + 1))
                if dd >= 2:
                    tn.wait_ge(s_oe.h, dd - 1)
                for k in range(NCH):
                    for hh in range(2):
                        cl = slice(hh * 512, (hh + 1) * 512)
                        ins = tn.matmul(P[dd % 2][:, cl], wr2[dd % 4][:, k * 128:(k + 1) * 128],
                                        att[k][:, cl], start=(k == 0), stop=(k == NCH - 1))
                ins.then_inc(s_om.inc(), 1)

        @blk.scalar
        def _(sc):
            for dd in range(NCH):
                sc.wait_ge(s_om.h, dd + 1)
                if dd >= 1:
                    sc.wait_ge(s_od.h, dd)
                sc.activation(ot[:], P[dd % 2][:], AF.Identity,
                              bias=bo_sb[:, dd:dd + 1]).then_inc(s_oe.inc(), 1)

        @blk.vector
        def _(v):
            for dd in range(NCH):
                v.wait_ge(s_oe.h, dd + 1)
                v.tensor_tensor(X[dd][:], X[dd][:].bitcast(F32), ot[:], ALU.add) \
                    .then_inc(s_od.inc(), 1)

    es_att.close()
    emit_store_stream(C, X, d_y, s_st, waits=[(s_od, s_od.n)])
    return nc, ["xT", "wqT", "wkT", "wvT", "woT", "wmqT", "wmkT", "bq", "bk", "bo",
                "bvR", "ones_col", "ones_row", "ident"]



def build_l0attn():
    import contextlib
    nc = bass.Bass("TRN2", target_bir_lowering=False)
    C = Ctx(nc)
    P = C.P
    d_x = nc.dram_tensor("xT", [D, TOK], F32, kind="ExternalInput").ap()
    d_y = nc.dram_tensor("yT", [D, TOK], F32, kind="ExternalOutput").ap()
    d_xpj = nc.dram_tensor("xprojT", [D, 160], F32, kind="ExternalInput").ap()
    d_dtp = nc.dram_tensor("dtprojT_sl", [128, DSL], F32, kind="ExternalInput").ap()
    d_dtb = nc.dram_tensor("dtb_sl", [128, 2], F32, kind="ExternalInput").ap()
    d_alog = nc.dram_tensor("alogp", [128, 32], F32, kind="ExternalInput").ap()
    d_D = nc.dram_tensor("D_sb", [128, 16], F32, kind="ExternalInput").ap()
    d_gp = nc.dram_tensor("g_pat", [16 * 128, 128], F32, kind="ExternalInput").ap()
    d_eb = nc.dram_tensor("e_b", [32, 128], F32, kind="ExternalInput").ap()
    d_ec = nc.dram_tensor("e_c", [32, 128], F32, kind="ExternalInput").ap()
    d_rp = nc.dram_tensor("r_pat", [128, 8], F32, kind="ExternalInput").ap()
    agx_in = nc.dram_tensor("agx_in", [D, TOK], F32)
    agx_out = nc.dram_tensor("agx_out", [NCORE * D, TOK], F32, addr_space="Shared")
    agd_in = nc.dram_tensor("agd_in", [160, TOK], F32)
    agd_out = nc.dram_tensor("agd_out", [NCORE * 160, TOK], F32, addr_space="Shared")
    y_mine = nc.dram_tensor("y_mine", [DSL, NTOK], F32)
    y_full = nc.dram_tensor("y_full", [D, NTOK], F32, addr_space="Shared")

    X = [sb(nc, f"x{c}", [128, 1024], F32R) for c in range(NCH)]
    gp_sb = sb(nc, "gp", [128, 16 * 128], F32R)
    eb_sb = sb(nc, "eb", [32, 128], F32R)
    ec_sb = sb(nc, "ec", [32, 128], F32R)
    rp_sb = sb(nc, "rp", [128, 8], F32R)
    dtp_sb = sb(nc, "dtp", [128, DSL], F32R)
    dtb_sb = sb(nc, "dtb", [128, 2], F32)
    alog_sb = sb(nc, "alog", [128, 32], F32)
    apat_sb = sb(nc, "apat", [128, 32], F32)
    D_sb = sb(nc, "Dsb", [128, 16], F32)
    carry = sb(nc, "carry", [128, 32], F32)
    jnk_sc = sb(nc, "jnksc", [128, 1], F32)

    s_ld = C.sem("ld"); s_st = C.sem("st")
    s_ag = C.sem("ag"); s_cc = C.sem("cc")
    s_u = C.sem("su"); s_uf = C.sem("suf")
    s_dr = C.sem("sdr")
    s_gdp = C.sem("gdp"); s_gda = C.sem("gda")
    s_gup = C.sem("gup"); s_gbu = C.sem("gbu")
    s_gsc = C.sem("gsc"); s_gtm = C.sem("gtm")
    s_gym = C.sem("gym"); s_gye = C.sem("gye"); s_gyd = C.sem("gyd")
    s_yl = C.sem("yl")

    with nc.Block() as blk:
        @blk.sync
        def _(sy):
            emit_load_stream(C, X, d_x, sy, s_ld)
            dmai(sy, gp_sb[:].rearrange("d (g p) -> d g p", g=16),
                 d_gp.rearrange("(g d) p -> d g p", d=128).bitcast(F32R), s_ld)
            dmai(sy, eb_sb[:], d_eb.bitcast(F32R), s_ld)
            dmai(sy, ec_sb[:], d_ec.bitcast(F32R), s_ld)
            dmai(sy, rp_sb[:], d_rp.bitcast(F32R), s_ld)
            dmai(sy, dtp_sb[:], d_dtp.bitcast(F32R), s_ld)
            dmai(sy, dtb_sb[:], d_dtb, s_ld)
            dmai(sy, alog_sb[:], d_alog, s_ld)
            dmai(sy, D_sb[:], d_D, s_ld)
            C.consts = _consts_load(C, sy, s_ld, ["ones_col", "ones_row"])
            sy.wait_ge(s_ld.h, s_ld.n)

    with nc.Block() as blk:
        @blk.scalar
        def _(sc):
            sc.activation(apat_sb[:], alog_sb[:], AF.Exp)
            sc.memzero(C.jC[:])

    with nc.Block() as blk:
        @blk.vector
        def _(v):
            v.tensor_scalar(apat_sb[:], apat_sb[:], -1.0, None, ALU.mult)
            v.memset(C.jC[:], 0.0)

    es = contextlib.ExitStack()
    XN = [es.enter_context(nc.sbuf_tensor(f"xn{c}", [128, 1024], F32R)) for c in range(NCH)]
    emit_ln(C, X, XN, C.consts, "lnm")

    es2 = contextlib.ExitStack()
    xpj_sb = es2.enter_context(nc.sbuf_tensor("xpj", [128, 16 * 160], F32R))
    dr_sb = es2.enter_context(nc.sbuf_tensor("drs", [128, 1024], F32R))
    bc_sb = es2.enter_context(nc.sbuf_tensor("bcs", [32, 1024], F32R))
    with nc.Block() as blk:
        @blk.sync
        def _(sy):
            dmai(sy, xpj_sb[:].rearrange("p (c m) -> p c m", c=16),
                 d_xpj.rearrange("(c p) m -> p c m", p=128).bitcast(F32R), s_ld)
            sy.wait_ge(s_ld.h, s_ld.n)

    with nc.Block() as blk:
        @blk.tensor
        def _(tn):
            for c in range(NCH):
                for hh in range(2):
                    cl = slice(hh * 512, (hh + 1) * 512)
                    tn.matmul(P[0][:, cl], xpj_sb[:, c * 160:c * 160 + 128],
                              XN[c][:, cl], start=(c == 0), stop=(c == NCH - 1))
                    tn.matmul(P[1][0:32, cl], xpj_sb[:, c * 160 + 128:c * 160 + 160],
                              XN[c][:, cl], start=(c == 0), stop=(c == NCH - 1))

    with nc.Block() as blk:
        @blk.vector
        def _(v):
            v.tensor_copy(dr_sb[:], P[0][:])
            v.tensor_copy(bc_sb[:], P[1][0:32, :])
            v.memset(C.jC[:], 0.0)

    with nc.Block() as blk:
        @blk.sync
        def _(sy):
            xv = agx_in.ap().rearrange("(c p) t -> c p t", p=128)
            for c in range(NCH):
                dmai(sy, xv[c], XN[c][:].bitcast(F32), s_ag)
            dmai(sy, agd_in.ap()[0:128, :], dr_sb[:].bitcast(F32), s_ag)
            dmai(sy, agd_in.ap()[128:160, :], bc_sb[:].bitcast(F32), s_ag)

        @blk.gpsimd
        def _(g):
            g.wait_ge(s_ag.h, s_ag.n)
            g.collective_compute("AllGather", ALU.bypass,
                                 replica_groups=[list(range(NCORE))],
                                 ins=[agx_in.ap()], outs=[agx_out.ap()]) \
                .then_inc(s_cc.inc(), 1)
            g.collective_compute("AllGather", ALU.bypass,
                                 replica_groups=[list(range(NCORE))],
                                 ins=[agd_in.ap()], outs=[agd_out.ap()]) \
                .then_inc(s_cc.inc(), 1)
            g.wait_ge(s_cc.h, 2)

    es2.close()
    es.close()

    u_t = [[es.enter_context(nc.sbuf_tensor(f"u{s}_{dh}", [128, 1024], F32))
            for dh in range(2)] for s in range(2)]
    drt = [es.enter_context(nc.sbuf_tensor(f"drt{s}", [128, 1024], F32R)) for s in range(2)]
    bct = [es.enter_context(nc.sbuf_tensor(f"bct{s}", [32, 1024], F32R)) for s in range(2)]
    zt = [es.enter_context(nc.sbuf_tensor(f"zt{dh}", [128, 1024], F32)) for dh in range(2)]
    ab2 = [es.enter_context(nc.sbuf_tensor(f"ab2{dh}", [128, 1024], F32)) for dh in range(2)]
    el = [es.enter_context(nc.sbuf_tensor(f"el{dh}", [128, 1024], F32)) for dh in range(2)]
    delta = [es.enter_context(nc.sbuf_tensor(f"dl{dh}", [128, 1024], F32R)) for dh in range(2)]
    du = [es.enter_context(nc.sbuf_tensor(f"du{dh}", [128, 1024], F32R)) for dh in range(2)]
    Bp = es.enter_context(nc.sbuf_tensor("Bp", [128, 1024], F32))
    Cp = es.enter_context(nc.sbuf_tensor("Cp", [128, 1024], F32))
    dA = [es.enter_context(nc.sbuf_tensor(f"dA{s}", [128, 1024], F32)) for s in range(2)]
    dBu = [es.enter_context(nc.sbuf_tensor(f"dBu{s}", [128, 1024], F32)) for s in range(2)]
    h_t = [es.enter_context(nc.sbuf_tensor(f"h{s}", [128, 1024], F32)) for s in range(2)]
    tmp = [es.enter_context(nc.sbuf_tensor(f"tmp{s}", [128, 1024], F32R)) for s in range(2)]
    y_sb = [es.enter_context(nc.sbuf_tensor(f"ysb{s}", [8, 1024], F32)) for s in range(2)]

    NTC = 8
    for tc in range(NTC):
        for arm in nc.switch_core_id(engines=[nc.sync], n=128):
            try:
                rank = int(arm.logical) % NCORE
            except Exception:
                rank = 0
            if tc >= 2:
                nc.sync.wait_ge(s_uf.h, tc - 1)
            for dh in range(2):
                nc.sync.dma_start(
                    out=u_t[tc % 2][dh][:],
                    in_=agx_out.ap()[tc * D + rank * DSL + dh * 128:
                                     tc * D + rank * DSL + (dh + 1) * 128, :]) \
                    .then_inc(s_u.h, 16)
        s_u.n += 32

        with nc.Block() as blk:
            @blk.sync
            def _(sy):
                dmai(sy, drt[tc % 2][:], agd_out.ap()[tc * 160:tc * 160 + 128, :]
                     .bitcast(F32R), s_dr)
                dmai(sy, bct[tc % 2][:], agd_out.ap()[tc * 160 + 128:tc * 160 + 160, :]
                     .bitcast(F32R), s_dr)
                sy.wait_ge(s_dr.h, s_dr.n)

        with nc.Block() as blk:
            @blk.tensor
            def _(tn):
                for dh in range(2):
                    for hh in range(2):
                        cl = slice(hh * 512, (hh + 1) * 512)
                        tn.matmul(P[dh][:, cl], dtp_sb[:, dh * 128:(dh + 1) * 128],
                                  drt[tc % 2][:, cl], start=True, stop=True)
                for hh in range(2):
                    cl = slice(hh * 512, (hh + 1) * 512)
                    tn.matmul(P[2][:, cl], eb_sb[:], bct[tc % 2][0:32, cl],
                              start=True, stop=True)
                    tn.matmul(P[3][:, cl], ec_sb[:], bct[tc % 2][0:32, cl],
                              start=True, stop=True)

        with nc.Block() as blk:
            @blk.scalar
            def _(sc):
                for dh in range(2):
                    sc.activation(zt[dh][:], P[dh][:], AF.Identity,
                                  bias=dtb_sb[:, dh:dh + 1])
                sc.memzero(C.jC[:])

        with nc.Block() as blk:
            @blk.vector
            def _(v):
                for dh in range(2):
                    v.scalar_tensor_tensor(ab2[dh][:], zt[dh][:], -1.0, zt[dh][:],
                                           ALU.mult, ALU.max)
                v.tensor_copy(Bp[:], P[2][:])
                v.tensor_copy(Cp[:], P[3][:])
                v.memset(C.jC[:], 0.0)

        with nc.Block() as blk:
            @blk.scalar
            def _(sc):
                for dh in range(2):
                    sc.activation(el[dh][:], ab2[dh][:], AF.Exp, scale=-1.0)
                for dh in range(2):
                    sc.activation(ab2[dh][:], el[dh][:], AF.Ln, bias=1.0)
                sc.memzero(C.jC[:])

        with nc.Block() as blk:
            @blk.vector
            def _(v):
                v.wait_ge(s_u.h, 32 * (tc + 1))
                for dh in range(2):
                    v.scalar_tensor_tensor(delta[dh][:], zt[dh][:], 0.0, ab2[dh][:],
                                           ALU.max, ALU.add)
                    ins = v.tensor_tensor(du[dh][:], delta[dh][:].bitcast(F32),
                                          u_t[tc % 2][dh][:], ALU.mult)
                    if dh == 1:
                        ins.then_inc(s_uf.inc(), 1)
                v.memset(C.jC[:], 0.0)

        with nc.Block() as blk:
            @blk.tensor
            def _(tn):
                for g in range(33):
                    gq = tc * 32 + g
                    if g < 32:
                        dh = g // 16
                        gl = g % 16
                        for hh in range(2):
                            q2 = 2 * gq + hh
                            cl = slice(hh * 512, (hh + 1) * 512)
                            if q2 >= 2:
                                tn.wait_ge(s_gda.h, q2 - 1)
                            tn.matmul(P[0][:, cl], gp_sb[:, gl * 128:(gl + 1) * 128],
                                      delta[dh][:, cl], start=True, stop=True) \
                                .then_inc(s_gdp.inc(), 1)
                            if q2 >= 2:
                                tn.wait_ge(s_gbu.h, q2 - 1)
                            tn.matmul(P[1][:, cl], gp_sb[:, gl * 128:(gl + 1) * 128],
                                      du[dh][:, cl], start=True, stop=True) \
                                .then_inc(s_gup.inc(), 1)
                    if g >= 1:
                        gy = gq - 1
                        tn.wait_ge(s_gtm.h, gy + 1)
                        if gy >= 2:
                            tn.wait_ge(s_gye.h, gy - 1)
                        for hh in range(2):
                            cl = slice(hh * 512, (hh + 1) * 512)
                            ins = tn.matmul(P[2 + (gy % 2)][0:8, cl], rp_sb[:],
                                            tmp[gy % 2][:, cl], start=True, stop=True)
                        ins.then_inc(s_gym.inc(), 1)

            @blk.scalar
            def _(sc):
                for g in range(33):
                    gq = tc * 32 + g
                    if g < 32:
                        for hh in range(2):
                            q2 = 2 * gq + hh
                            cl = slice(hh * 512, (hh + 1) * 512)
                            sc.wait_ge(s_gdp.h, q2 + 1)
                            if gq >= 2:
                                sc.wait_ge(s_gsc.h, gq - 1)
                            sc.activation(dA[gq % 2][:, cl], P[0][:, cl], AF.Exp,
                                          scale=apat_sb[:, g:g + 1]) \
                                .then_inc(s_gda.inc(), 1)
                    if g >= 1:
                        gy = gq - 1
                        sc.wait_ge(s_gym.h, gy + 1)
                        if gy >= 2:
                            sc.wait_ge(s_gyd.h, 16 * (gy - 1))
                        sc.activation(y_sb[gy % 2][:], P[2 + (gy % 2)][0:8, :],
                                      AF.Identity).then_inc(s_gye.inc(), 1)

            @blk.vector
            def _(v):
                for g in range(32):
                    gq = tc * 32 + g
                    for hh in range(2):
                        q2 = 2 * gq + hh
                        cl = slice(hh * 512, (hh + 1) * 512)
                        v.wait_ge(s_gup.h, q2 + 1)
                        if gq >= 2:
                            v.wait_ge(s_gsc.h, gq - 1)
                        v.tensor_tensor(dBu[gq % 2][:, cl], P[1][:, cl], Bp[:, cl],
                                        ALU.mult).then_inc(s_gbu.inc(), 1)
                    v.wait_ge(s_gda.h, 2 * gq + 2)
                    ini = 0.0 if tc in (0, 4) else carry[:, g:g + 1]
                    v.tensor_tensor_scan(h_t[gq % 2][:], dA[gq % 2][:], dBu[gq % 2][:],
                                         ini, ALU.mult, ALU.add) \
                        .then_inc(C.sem("hz").inc(), 1)
                    v.wait_ge(C.sem("hz").h, C.sem("hz").n)
                    v.tensor_copy(carry[:, g:g + 1], h_t[gq % 2][:, 1023:1024]) \
                        .then_inc(s_gsc.inc(), 1)
                    v.tensor_tensor(tmp[gq % 2][:], h_t[gq % 2][:], Cp[:], ALU.mult) \
                        .then_inc(s_gtm.inc(), 1)

            @blk.sync
            def _(sy):
                for g in range(32):
                    gq = tc * 32 + g
                    sy.wait_ge(s_gye.h, gq + 1)
                    sy.dma_start(out=y_mine.ap()[8 * g:8 * g + 8,
                                                 tc * 1024:(tc + 1) * 1024],
                                 in_=y_sb[gq % 2][:]).then_inc(s_gyd.inc(16), 16)

    with nc.Block() as blk:
        @blk.gpsimd
        def _(g):
            g.wait_ge(s_gyd.h, s_gyd.n)
            g.collective_compute("AllGather", ALU.bypass,
                                 replica_groups=[list(range(NCORE))],
                                 ins=[y_mine.ap()], outs=[y_full.ap()]) \
                .then_inc(s_cc.inc(), 1)
            g.wait_ge(s_cc.h, 3)

    es.close()
    yloc = [es.enter_context(nc.sbuf_tensor(f"yl{c}", [128, 1024], F32))
            for c in range(NCH)]
    xnl = [es.enter_context(nc.sbuf_tensor(f"xnl{c}", [128, 1024], F32))
           for c in range(NCH)]

    for arm in nc.switch_core_id(engines=[nc.sync], n=128):
        try:
            rank = int(arm.logical) % NCORE
        except Exception:
            rank = 0
        sv = s_yl.n
        for c in range(NCH):
            nc.sync.dma_start(out=yloc[c][:],
                              in_=y_full.ap()[c * 128:(c + 1) * 128,
                                              rank * 1024:(rank + 1) * 1024]) \
                .then_inc(s_yl.h, 16)
            nc.sync.dma_start(out=xnl[c][:],
                              in_=agx_out.ap()[rank * D + c * 128:
                                               rank * D + (c + 1) * 128, :]) \
                .then_inc(s_yl.h, 16)
        s_yl.n = sv
    s_yl.n += 32 * NCH

    with nc.Block() as blk:
        @blk.vector
        def _(v):
            v.wait_ge(s_yl.h, s_yl.n)
            s_fin = C.sem("fin")
            for c in range(NCH):
                v.scalar_tensor_tensor(yloc[c][:], xnl[c][:], D_sb[:, c:c + 1],
                                       yloc[c][:], ALU.mult, ALU.add)
                v.tensor_tensor(X[c][:], X[c][:].bitcast(F32), yloc[c][:], ALU.add) \
                    .then_inc(s_fin.inc(), 1)

    es.close()
    emit_store_stream(C, X, d_y, s_st, waits=[(C.sem("fin"), NCH)])
    return nc, ["xT", "xprojT", "dtprojT_sl", "dtb_sl", "alogp", "D_sb", "g_pat",
                "e_b", "e_c", "r_pat", "ones_col", "ones_row"]



_CACHE = {}
_PERCORE = {"dtprojT_sl", "dtb_sl", "alogp"}


def kernel(**inputs):
    h = _prep(inputs)
    if "progs" not in _CACHE:
        _CACHE["progs"] = {
            "l0ffn": build_l0ffn(),
            "l0attn": build_l0attn(),
            "l1ffn": build_l1ffn(),
            "l1attn": build_l1attn(),
        }
    progs = _CACHE["progs"]
    state = [np.ascontiguousarray(h["xT"][:, c * TOK:(c + 1) * TOK]) for c in range(NCORE)]

    def run(pname):
        nc, keys = progs[pname]
        in_maps = []
        for c in range(NCORE):
            m = {}
            for k in keys:
                if k == "xT":
                    m[k] = state[c]
                elif k in _PERCORE:
                    m[k] = h[k][c]
                else:
                    m[k] = h[k]
            in_maps.append(m)
        res = run_bass_kernel_spmd(nc, in_maps, core_ids=list(range(NCORE)))
        for c in range(NCORE):
            state[c] = np.ascontiguousarray(res.results[c]["yT"])

    run("l0ffn")
    run("l0attn")
    run("l0ffn")
    run("l1ffn")
    run("l1attn")
    run("l1ffn")

    full = np.concatenate(state, axis=1)
    return np.ascontiguousarray(full.T).reshape(2, 4096, D).astype(np.float32)
